# version 02 (final)
# baseline (speedup 1.0000x reference)
import numpy as np
import ml_dtypes
import concourse.bass as bass
import concourse.mybir as mybir
from concourse.bass_utils import run_bass_kernel_spmd
from contextlib import ExitStack

F32 = mybir.dt.float32
BF16 = mybir.dt.bfloat16
AF = mybir.ActivationFunctionType
ALU = mybir.AluOpType
BF = ml_dtypes.bfloat16

NCORES = 8
D = 4096
S = 8192
TOK = 1024
HALO = 3
T = TOK + HALO
E_PROJ = 18560
INTER = 8192
CONV_DIM = 10240
H, P, N, G, KCONV = 128, 64, 128, 8, 4
NZ = 8
CHK = 128
LN_EPS = 1e-5
GN_EPS = 1e-5
DT_MAX = 100.0
NEGM = -1e30

DT_ND = 32
ET_FFN = 32
ET_PROJ = 145
ET_OUT = 64

ENGINES = ("sync", "vector", "scalar", "tensor", "gpsimd")
DEV_PHASES = None
DEV_DEBUG_OUT = False
DEV_NO_RACEDET = False


class KB:
    def __init__(self, nc, tensors, sems):
        self.nc = nc
        self.t = tensors
        self.sem = sems
        self.cur = None
        self.eng = None
        self.cnt = {}

    def reset(self, engname):
        self.cur = engname
        self.eng = getattr(self.nc, engname)
        for s in self.sem:
            self.cnt[s] = 0

    def inc(self, ins, sem, n):
        self.cnt[sem] += n
        if ins is not None:
            ins.then_inc(self.sem[sem], n)
        return self.cnt[sem]

    def val(self, sem):
        return self.cnt[sem]

    def wait(self, engine, sem, v):
        if v <= 0:
            return
        if self.cur == engine:
            self.eng.wait_ge(self.sem[sem], v)

    def on(self, engine):
        return self.cur == engine

    def dma(self, q, out, in_, sem=None, n=16):
        ins = self.eng.dma_start(out=out, in_=in_) if self.cur == q else None
        if sem is not None:
            return self.inc(ins, sem, n)
        return None

    def mm(self, out, lhsT, rhs, start, stop, sem=None, n=1):
        ins = self.eng.matmul(out=out, lhsT=lhsT, rhs=rhs, start=start, stop=stop) if self.cur == "tensor" else None
        if sem is not None:
            return self.inc(ins, sem, n)
        return None

    def trans(self, out, in_, ident, sem=None, n=1):
        ins = self.eng.transpose(out=out, in_=in_, identity=ident) if self.cur == "tensor" else None
        if sem is not None:
            return self.inc(ins, sem, n)
        return None

    def act(self, out, in_, func=AF.Copy, bias=0.0, scale=1.0, sem=None, n=1):
        ins = self.eng.activation(out=out, in_=in_, func=func, bias=bias, scale=scale) if self.cur == "scalar" else None
        if sem is not None:
            return self.inc(ins, sem, n)
        return None

    def dve(self, fn, sem=None, n=1):
        ins = fn(self.eng) if self.cur == "vector" else None
        if sem is not None:
            return self.inc(ins, sem, n)
        return None

    def pool(self, fn, sem=None, n=1):
        ins = fn(self.eng) if self.cur == "gpsimd" else None
        if sem is not None:
            return self.inc(ins, sem, n)
        return None


def host_prep(inputs):
    x = np.asarray(inputs["L0_attn_x"])
    ipw = np.asarray(inputs["L0_attn_in_proj_w"])
    convw = np.asarray(inputs["L0_attn_conv_w"])
    convb = np.asarray(inputs["L0_attn_conv_b"])
    dtb = np.asarray(inputs["L0_attn_dt_bias"])
    alog = np.asarray(inputs["L0_attn_A_log"])
    Dp = np.asarray(inputs["L0_attn_D"])
    normw = np.asarray(inputs["L0_attn_norm_w"])
    opw = np.asarray(inputs["L0_attn_out_proj_w"])
    W = np.asarray(inputs["L0_ffn_W"])
    lA = np.asarray(inputs["L0_ffn_lora_A"])
    lB = np.asarray(inputs["L0_ffn_lora_B"])
    mag = np.asarray(inputs["L0_ffn_magnitude"])

    xT = np.zeros((D, S + HALO), dtype=np.float32)
    xT[:, HALO:] = x[0].T

    shared = {
        "inproj_wT": np.ascontiguousarray(ipw.T),
        "outproj_wT": np.ascontiguousarray(opw.T),
        "ffn_wT": np.ascontiguousarray(W.T),
        "lora_A": np.ascontiguousarray(lA),
        "lora_AT": np.ascontiguousarray(lA.T),
        "lora_BT": np.ascontiguousarray(lB.T),
        "lora_B_nat": np.ascontiguousarray(lB),
        "mag_col": np.ascontiguousarray(mag.reshape(32, 128).T),
        "convb_col": np.ascontiguousarray(convb.reshape(80, 128).T),
        "convw_col": np.ascontiguousarray(convw[:, 0, :].reshape(80, 128, 4).transpose(1, 2, 0)),
        "dtb_col": np.ascontiguousarray(dtb.reshape(128, 1)),
        "alog_col": np.ascontiguousarray(alog.reshape(128, 1)),
        "d_col": np.ascontiguousarray(Dp.reshape(128, 1)),
        "d_row": np.ascontiguousarray(Dp.reshape(1, 128)),
        "normw_col": np.ascontiguousarray(normw.reshape(64, 128).T),
        "normw_row": np.ascontiguousarray(normw.reshape(1, INTER)),
    }
    per_core = []
    for c in range(NCORES):
        m = dict(shared)
        m["xT"] = np.ascontiguousarray(xT[:, TOK * c : TOK * c + T])
        per_core.append(m)
    return per_core


CO_MAG = 0
CO_CONVB = 32
CO_DTB = 112
CO_ALOG = 113
CO_DCOL = 114
CO_NORMW = 120
CO_EPS = 115
CO_ONE = 116
CO_NEGHALF = 117


def build_program(mode="L1"):
    nc = bass.Bass("TRN2", target_bir_lowering=False, debug=False,
                   detect_race_conditions=not DEV_NO_RACEDET)
    t = {}

    def ein(name, shape, dt=F32):
        t[name] = nc.dram_tensor(name, list(shape), dt, kind="ExternalInput").ap()

    ein("xT", (D, T))
    ein("inproj_wT", (D, E_PROJ))
    ein("outproj_wT", (INTER, D))
    ein("ffn_wT", (D, D))
    ein("lora_A", (32, D))
    ein("lora_AT", (D, 32))
    ein("lora_BT", (32, D))
    ein("lora_B_nat", (D, 32))
    ein("mag_col", (128, 32))
    ein("convb_col", (128, 80))
    ein("convw_col", (128, 4, 80))
    ein("dtb_col", (128, 1))
    ein("alog_col", (128, 1))
    ein("d_col", (128, 1))
    ein("d_row", (1, 128))
    ein("normw_col", (128, 64))
    ein("normw_row", (1, INTER))

    t["out_fm"] = nc.dram_tensor("out_fm", [D, TOK], F32, kind="ExternalOutput").ap()
    RELAY = ("scr_x", "scr_hs_tm", "scr_gate_tm", "scr_B_fm", "scr_C_fm",
             "scr_acum", "scr_actm", "scr_states", "scr_dt", "scr_wb", "scr_hscale")
    relay_kind = {"L1": "ExternalOutput", "L2": "ExternalInput"}.get(mode, "Internal")
    if mode == "L1":
        t["out_state"] = nc.dram_tensor("out_state", [128, INTER], BF16, kind="ExternalOutput").ap()
        t["out_dseg"] = nc.dram_tensor("out_dseg", [128, 1], F32, kind="ExternalOutput").ap()
    if mode == "L2":
        t["state_all"] = nc.dram_tensor("state_all", [NCORES, 128, INTER], BF16, kind="ExternalInput").ap()
        t["dseg_all"] = nc.dram_tensor("dseg_all", [128, NCORES], F32, kind="ExternalInput").ap()
        t["segmask"] = nc.dram_tensor("segmask", [1, NCORES], F32, kind="ExternalInput").ap()
    if DEV_DEBUG_OUT:
        t["dbg_hs"] = nc.dram_tensor("dbg_hs", [128, 32], F32, kind="ExternalOutput").ap()
        t["dbg_rows"] = nc.dram_tensor("dbg_rows", [4, T], F32, kind="ExternalOutput").ap()
        t["dbg_u"] = nc.dram_tensor("dbg_u", [32, T], F32, kind="ExternalOutput").ap()
        t["dbg_xn"] = nc.dram_tensor("dbg_xn", [128, DT_ND * T], F32, kind="ExternalOutput").ap()
        t["dbg_mb"] = nc.dram_tensor("dbg_mb", [128, T], F32, kind="ExternalOutput").ap()
        t["dbg_rb"] = nc.dram_tensor("dbg_rb", [128, T], F32, kind="ExternalOutput").ap()
        t["dbg_t1"] = nc.dram_tensor("dbg_t1", [128, 32], F32, kind="ExternalOutput").ap()
        t["dbg_wat"] = nc.dram_tensor("dbg_wat", [128, 64], F32, kind="ExternalOutput").ap()
        if mode == "L2":
            t["dbg_yz"] = nc.dram_tensor("dbg_yz", [128, INTER], F32, kind="ExternalOutput").ap()
            t["dbg_lt"] = nc.dram_tensor("dbg_lt", [128, 2048], BF16, kind="ExternalOutput").ap()
            t["dbg_sc"] = nc.dram_tensor("dbg_sc", [128, 1024], F32, kind="ExternalOutput").ap()
            t["dbg_bc"] = nc.dram_tensor("dbg_bc", [128, 2048], BF16, kind="ExternalOutput").ap()
            t["dbg_hst"] = nc.dram_tensor("dbg_hst", [128, INTER], BF16, kind="ExternalOutput").ap()
            t["dbg_dbc"] = nc.dram_tensor("dbg_dbc", [128, INTER], BF16, kind="ExternalOutput").ap()

    def scr(name, shape, dt):
        kind = relay_kind if name in RELAY else "Internal"
        if DEV_DEBUG_OUT and mode == "L2" and name == "scr_yn":
            kind = "ExternalOutput"
        t[name] = nc.dram_tensor(name, list(shape), dt, kind=kind).ap()

    scr("scr_x", (D, T), F32)
    scr("scr_row", (4, T), F32)
    scr("scr_wb", (ET_FFN, 128, D), BF16)
    scr("scr_hs_tm", (TOK, INTER), BF16)
    scr("scr_gate_tm", (TOK, INTER), BF16)
    scr("scr_B_fm", (G * N, TOK), BF16)
    scr("scr_C_fm", (G * N, TOK), BF16)
    scr("scr_B_tm", (TOK, G * N), BF16)
    scr("scr_acum", (NZ, 128, 128), F32)
    scr("scr_actm", (NZ, 128, 128), F32)
    scr("scr_dt", (128, 2048), F32)
    scr("scr_hscale", (128, 32), F32)
    scr("scr_states", (NZ, N, INTER), BF16)
    scr("scr_yn", (ET_OUT, 4, 128, 256), BF16)
    scr("scr_dexp", (128, 64), BF16)

    t["c_ident_bf"] = nc.inline_tensor(np.eye(128, dtype=BF), name="c_ident_bf").ap()
    t["c_ident_f"] = nc.inline_tensor(np.eye(128, dtype=np.float32), name="c_ident_f").ap()
    t["c_ones_col_bf"] = nc.inline_tensor(np.ones((128, 1), dtype=BF), name="c_ones_col_bf").ap()
    t["c_ones32_col_bf"] = nc.inline_tensor(np.ones((32, 1), dtype=BF), name="c_ones32_col_bf").ap()
    t["c_ones_row_f"] = nc.inline_tensor(np.ones((1, 128), dtype=np.float32), name="c_ones_row_f").ap()
    t["c_cumU_f"] = nc.inline_tensor(np.triu(np.ones((128, 128), dtype=np.float32)), name="c_cumU_f").ap()
    nm = np.where(np.arange(128)[None, :] >= np.arange(128)[:, None], 0.0, NEGM).astype(np.float32)
    t["c_negmask_f"] = nc.inline_tensor(nm, name="c_negmask_f").ap()

    es = ExitStack()

    def sb(name, shape, dt):
        t[name] = es.enter_context(nc.sbuf_tensor(name, list(shape), dt))

    def psum(name, shape, dt=F32):
        t[name] = es.enter_context(nc.psum_tensor(name, list(shape), dt))

    sb("consts", (128, 200), F32)
    sb("ident_bf", (128, 128), BF16)
    sb("ident_f", (128, 128), F32)
    sb("cumU_f", (128, 128), F32)
    sb("negmask_f", (128, 128), F32)
    sb("ones_col_bf", (128, 1), BF16)
    sb("ones32_col_bf", (32, 1), BF16)
    sb("ones_row_f", (1, 128), F32)

    sb("convw_sb", (128, 4, 80), F32)
    sb("halfscale", (128, 32), F32)
    sb("AT_bf", (128, DT_ND * 32), BF16)
    sb("BT_bf", (32, D), BF16)
    sb("G_bf", (32, 32), BF16)
    sb("BTT_sb", (128, 32 * 32), BF16)
    sb("GBT_sb", (128, 32 * 32), F32)
    sb("u_bf", (32, T), BF16)
    sb("mrow", (1, T), F32)
    sb("qrow", (1, T), F32)
    sb("srow", (1, T), F32)
    sb("rsrow", (1, T), F32)
    sb("watmp2", (128, 32), F32)
    sb("prodc", (128, 64), F32)
    sb("rowtmp", (1, 512), F32)
    sb("coltmp", (128, 16), F32)
    sb("t1col_sb", (128, 32), F32)
    sb("watcol_sb", (128, 64), F32)
    sb("arena", (128, 42500), F32)
    ar = t["arena"]
    def AV(off, ncols, dt=F32, shape=None):
        v = ar[:, off:off + ncols]
        if dt is not F32:
            v = v.bitcast(dt)
        if shape is not None:
            import math as _m
            v = v.rearrange("p (" + " ".join(chr(97 + i) for i in range(len(shape))) + ") -> p " +
                            " ".join(chr(97 + i) for i in range(len(shape))),
                            **{chr(97 + i): shape[i] for i in range(len(shape))})
        return v
    t["AV"] = AV
    t["mb"] = AV(37377, 1027)
    t["rb"] = AV(36350, 1027)
    t["sqb"] = AV(34302, 2048, BF16)
    t["wbuf"] = AV(38404, 4096, BF16, (2, DT_ND, 128))
    t["xn"] = AV(0, 16432, BF16)
    t["xf"] = AV(16432, 2054, F32, (2, T))
    t["xb_t"] = AV(18486, 1027, BF16, (2, T))
    t["xsq_t"] = AV(19514, 1027, BF16, (2, T))
    t["xres"] = AV(20542, 2054, F32, (2, T))
    t["xp"] = AV(22596, 2054, F32, (2, T))
    t["xpraw"] = AV(24650, 2054, F32, (2, T))
    t["hbcp"] = AV(16432, 1027, BF16, (2, T))
    t["tstg"] = AV(17459, 1024, BF16, (2, NZ, 128))
    t["diag"] = AV(18483, 256, BF16, (4, 128))
    t["dtt"] = AV(18739, 1027)
    t["spbuf"] = AV(19766, 2054, F32, (2, T))
    t["adt"] = AV(21820, 1024)
    t["gbuf"] = AV(22844, 1024, BF16, (2, 1024))
    t["dexps"] = AV(17459, 32, BF16)
    t["dcolmm"] = ar[0:1, 17523:17651]
    t["dtfm"] = AV(34302, 1024)
    t["lndt"] = AV(35326, 1024)
    t["acfm"] = AV(36350, 1024)
    t["actm"] = AV(37374, 1024)
    t["prev"] = AV(0, 8192)
    t["hstch"] = AV(8192, 8192, BF16, (2, INTER))
    t["hsc"] = AV(16384, 4096, BF16)
    t["stz"] = AV(20480, 8192, BF16, (2, INTER))
    t["btmch"] = AV(28672, 1024, BF16, (2, 1024))
    t["ctm1"] = AV(29696, 128)
    t["ctm2"] = AV(29824, 128)
    t["ctm3"] = AV(29952, 128)
    t["erow"] = ar[0:1, 33835:33963]
    t["ecol"] = AV(33963, 1)
    t["esb"] = AV(33964, 128)
    t["dscol"] = AV(34092, 1)
    t["prevb"] = AV(8192, 4096, BF16)
    t["ystg"] = AV(8192, 4096, BF16)
    t["hstch1"] = AV(12288, 4096, BF16)
    t["gtch1"] = AV(16384, 4096, BF16)
    t["stzld"] = AV(20480, 4096, BF16)
    t["y_z"] = AV(24576, 4096, BF16)
    t["dbc"] = AV(28672, 4096, BF16)
    t["bfm"] = AV(32768, 512, BF16, (8, 128))
    t["cfm"] = AV(33280, 512, BF16, (8, 128))
    t["d1stg"] = AV(38404, 1024, F32, (2, 512))
    t["lstage"] = AV(39428, 1024, F32, (2, 512))
    t["m1buf"] = AV(40452, 128, BF16, (2, 128))
    t["pbias_tm"] = AV(40580, 128)
    t["lndt_tm"] = AV(40708, 128)
    t["expa"] = AV(40836, 128)
    t["t1buf"] = AV(40964, 512)
    t["lt16"] = AV(41476, 1024, BF16, (16, 128))
    t["gwork"] = AV(39428, 1024)
    t["sqw"] = AV(20480, 1024)
    t["grs"] = AV(33792, 16)
    t["dsegsb"] = AV(33808, 8)
    t["smaskb"] = AV(33816, 8)
    t["msml"] = AV(33824, 8)
    t["yres"] = AV(0, 32768, BF16)
    t["oxres"] = AV(32768, 2054, F32, (2, T))
    t["oxp"] = AV(34822, 2054, F32, (2, T))

    for _b in range(8):
        psum("ps%d" % _b, (128, 512))

    semnames = ("sWL", "sGP", "sPE", "sDV", "sSC", "sPL", "sST", "sAX",
                "sXF0", "sXF1", "sPL0", "sMB", "sWB0", "sWB1", "sWC", "sXR0", "sXR1",
                "sXP0", "sXP1", "sSP0", "sSP1", "sHS0", "sHS1", "sBT0", "sBT1",
                "sSZ0", "sSZ1", "sGT0", "sGT1", "sD10", "sD11", "sYS0", "sYS1", "sSO")
    sems = {s: es.enter_context(nc.semaphore(s)) for s in semnames}
    blk = es.enter_context(nc.Block())
    kb = KB(nc, t, sems)

    def program():
        emit_preload(kb)
        emit_ln_stats(kb, src="xT", q="sync", qsem="sWL")
        emit_ln_rows_and_normalize(kb, src="xT", q="sync", qsem="sWL")
        emit_ffn(kb, first=True, phases=None)
        if mode == "L1" and not (DEV_PHASES and "B" not in DEV_PHASES):
            emit_ln_stats(kb, src="scr_x", q="sync", qsem="sWL",
                          gates=[("sXP0", kb.state_ffn_xp0), ("sXP1", kb.state_ffn_xp1)])
            emit_ln_rows_and_normalize(kb, src="scr_x", q="sync", qsem="sWL",
                                       pe_guard=kb.state_ffn_pe)
            emit_inproj(kb)
            emit_chunk_prep(kb)
            emit_states(kb)
        emit_final(kb, mode)

    def program_l2():
        emit_preload(kb)
        emit_state_combine(kb)
        emit_scan(kb)
        emit_outproj(kb)
        emit_ln_stats(kb, src="scr_x", q="sync", qsem="sWL",
                      gates=[("sXP0", kb.state_ffn_xp0), ("sXP1", kb.state_ffn_xp1)])
        emit_ln_rows_and_normalize(kb, src="scr_x", q="sync", qsem="sWL",
                                   pe_guard=kb.state_ffn_pe)
        emit_ffn(kb, first=False, phases=None, final=True)
        emit_final(kb, mode)

    prog = program if mode != "L2" else program_l2
    with es:
        for engname in ENGINES:
            kb.reset(engname)
            getattr(blk, engname)(lambda _e: prog())
    return nc, t


def emit_preload(kb: KB):
    t = kb.t
    for dst, src in (
        (t["consts"][:, CO_MAG:CO_MAG + 32], t["mag_col"][:]),
        (t["consts"][:, CO_CONVB:CO_CONVB + 80], t["convb_col"][:]),
        (t["consts"][:, CO_DTB:CO_DTB + 1], t["dtb_col"][:]),
        (t["consts"][:, CO_ALOG:CO_ALOG + 1], t["alog_col"][:]),
        (t["consts"][:, CO_DCOL:CO_DCOL + 1], t["d_col"][:]),
        (t["consts"][:, CO_NORMW:CO_NORMW + 64], t["normw_col"][:]),
        (t["ident_bf"][:], t["c_ident_bf"][:]),
        (t["ident_f"][:], t["c_ident_f"][:]),
        (t["cumU_f"][:], t["c_cumU_f"][:]),
        (t["negmask_f"][:], t["c_negmask_f"][:]),
        (t["ones_col_bf"][:], t["c_ones_col_bf"][:]),
        (t["ones32_col_bf"][:], t["c_ones32_col_bf"][:]),
        (t["ones_row_f"][:], t["c_ones_row_f"][:]),
        (t["dcolmm"][:], t["d_row"][:]),
        (t["convw_sb"][:], t["convw_col"][:]),
    ):
        kb.dma("sync", dst, src, "sWL")
    kb.dma("gpsimd", t["AT_bf"][:].rearrange("p (t r) -> p t r", r=32),
           t["lora_AT"][:].rearrange("(t p) r -> p t r", p=128), "sPL0")
    kb.dma("gpsimd", t["BT_bf"][:], t["lora_BT"][:], "sPL0")
    kb.dma("gpsimd", t["BTT_sb"][:].rearrange("p (t r) -> p t r", r=32),
           t["lora_B_nat"][:].rearrange("(t p) r -> p t r", p=128), "sPL0")
    kb.dve(lambda e: e.memset(t["consts"][:, CO_EPS:CO_EPS + 1], LN_EPS))
    kb.dve(lambda e: e.memset(t["consts"][:, CO_ONE:CO_ONE + 1], 1.0))
    v = kb.dve(lambda e: e.memset(t["consts"][:, CO_NEGHALF:CO_NEGHALF + 1], -0.5), sem="sDV")
    kb.state_preload_wl = kb.val("sWL")
    kb.state_preload_gp = kb.val("sPL0")
    kb.state_preload_dv = v


def emit_ln_stats(kb: KB, src, q, qsem, gates=()):
    t = kb.t
    kb.wait("tensor", "sWL", kb.state_preload_wl)
    for gs, gv in gates:
        kb.wait(q, gs, gv)
    base_sc = kb.val("sSC")
    ld = []
    for i in range(DT_ND):
        par = i % 2
        if i >= 2:
            kb.wait(q, "sSC", base_sc + 2 * (i - 1))
        xs = "sXF%d" % par
        v = kb.dma(q, t["xf"][:, par, :], t[src][128 * i:128 * (i + 1), :], xs)
        ld.append(v)
        kb.wait("scalar", xs, v)
        kb.act(t["xb_t"][:, par, :], t["xf"][:, par, :], AF.Copy, sem="sSC")
        kb.act(t["xsq_t"][:, par, :], t["xf"][:, par, :], AF.Square, sem="sSC")
        kb.wait("tensor", "sSC", base_sc + 2 * i + 2)
        st = (i == 0)
        sp = (i == DT_ND - 1)
        kb.mm(t["ps2"][0:1, :], t["ones_col_bf"][:], t["xb_t"][:, par, 0:512], st, sp)
        kb.mm(t["ps3"][0:1, :], t["ones_col_bf"][:], t["xb_t"][:, par, 512:1024], st, sp)
        kb.mm(t["ps4"][0:1, 0:3], t["ones_col_bf"][:], t["xb_t"][:, par, 1024:T], st, sp)
        kb.mm(t["ps5"][0:1, :], t["ones_col_bf"][:], t["xsq_t"][:, par, 0:512], st, sp)
        kb.mm(t["ps6"][0:1, :], t["ones_col_bf"][:], t["xsq_t"][:, par, 512:1024], st, sp)
        last = kb.mm(t["ps7"][0:1, 0:3], t["ones_col_bf"][:], t["xsq_t"][:, par, 1024:T], st, sp,
                     sem="sPE" if sp else None)
    kb.state_ln_pe = kb.val("sPE")
    kb.state_ln_sc = kb.val("sSC")


def emit_ln_rows_and_normalize(kb: KB, src, q, qsem, pe_guard=0):
    t = kb.t
    kb.wait("scalar", "sPE", kb.state_ln_pe)
    inv = 1.0 / D
    kb.act(t["mrow"][0:1, 0:512], t["ps2"][0:1, :], AF.Copy, scale=inv, sem="sSC")
    kb.act(t["mrow"][0:1, 512:1024], t["ps3"][0:1, :], AF.Copy, scale=inv, sem="sSC")
    kb.act(t["mrow"][0:1, 1024:T], t["ps4"][0:1, 0:3], AF.Copy, scale=inv, sem="sSC")
    kb.act(t["qrow"][0:1, 0:512], t["ps5"][0:1, :], AF.Copy, scale=inv, sem="sSC")
    kb.act(t["qrow"][0:1, 512:1024], t["ps6"][0:1, :], AF.Copy, scale=inv, sem="sSC")
    v_sc = kb.act(t["qrow"][0:1, 1024:T], t["ps7"][0:1, 0:3], AF.Copy, scale=inv, sem="sSC")
    kb.wait("vector", "sSC", v_sc)
    v0 = kb.dve(lambda e: e.tensor_mul(out=t["srow"][0:1, :], in0=t["mrow"][0:1, :], in1=t["mrow"][0:1, :]), sem="sDV")
    kb.wait("vector", "sDV", v0)
    v_dv = kb.dve(lambda e: e.tensor_sub(out=t["qrow"][0:1, :], in0=t["qrow"][0:1, :], in1=t["srow"][0:1, :]), sem="sDV")
    kb.wait("scalar", "sDV", v_dv)
    kb.wait("scalar", "sDV", kb.state_preload_dv)
    v_sc = kb.act(t["srow"][0:1, :], t["qrow"][0:1, :], AF.Sqrt, bias=t["consts"][0:1, CO_EPS:CO_EPS + 1], sem="sSC")
    kb.wait("vector", "sSC", v_sc)
    v_dv = kb.dve(lambda e: e.reciprocal(out=t["rsrow"][0:1, :], in_=t["srow"][0:1, :]), sem="sDV")
    kb.wait("gpsimd", "sDV", v_dv)
    v1 = kb.dma("gpsimd", t["scr_row"][0:1, :], t["mrow"][0:1, :], "sMB")
    v2 = kb.dma("gpsimd", t["scr_row"][1:2, :], t["rsrow"][0:1, :], "sMB")
    kb.wait("gpsimd", "sMB", v2)
    v3 = kb.dma("gpsimd", t["mb"][:], t["scr_row"][0:1, :].partition_broadcast(128), "sMB")
    v4 = kb.dma("gpsimd", t["rb"][:], t["scr_row"][1:2, :].partition_broadcast(128), "sMB")
    base_dv = kb.val("sDV")
    for i in range(DT_ND):
        par = i % 2
        if i < 2:
            kb.wait(q, "sSC", kb.state_ln_sc)
        else:
            kb.wait(q, "sDV", base_dv + 2 * (i - 1))
        if i == 0:
            kb.wait("vector", "sPE", pe_guard)
        xs = "sXF%d" % par
        v = kb.dma(q, t["xf"][:, par, :], t[src][128 * i:128 * (i + 1), :], xs)
        kb.wait("vector", xs, v)
        if i == 0:
            kb.wait("vector", "sMB", v4)
        v_sub = kb.dve(lambda e, par=par: e.tensor_sub(out=t["xf"][:, par, :], in0=t["xf"][:, par, :], in1=t["mb"][:]), sem="sDV")
        kb.wait("vector", "sDV", v_sub)
        kb.dve(lambda e, par=par, i=i: e.tensor_mul(out=t["xn"][:, T * i:T * (i + 1)], in0=t["xf"][:, par, :], in1=t["rb"][:]), sem="sDV")
    kb.state_xn_dv = kb.val("sDV")
    kb.state_xn_dv_base = base_dv


def emit_ffn(kb: KB, first, phases, final=False):
    t = kb.t
    kb.wait("tensor", "sPL0", kb.state_preload_gp)
    kb.wait("tensor", "sWL", kb.state_preload_wl)
    v_pe = 0
    for i in range(DT_ND):
        kb.wait("tensor", "sDV", kb.state_xn_dv_base + 2 * (i + 1))
        st = (i == 0)
        sp = (i == DT_ND - 1)
        lhs = t["AT_bf"][:, 32 * i:32 * (i + 1)]
        kb.mm(t["ps0"][0:32, :], lhs, t["xn"][:, T * i:T * i + 512], st, sp)
        kb.mm(t["ps1"][0:32, :], lhs, t["xn"][:, T * i + 512:T * i + 1024], st, sp)
        v_pe = kb.mm(t["ps2"][0:32, 0:3], lhs, t["xn"][:, T * i + 1024:T * i + T], st, sp,
                     sem="sPE" if sp else None)
    kb.wait("scalar", "sPE", v_pe)
    kb.act(t["u_bf"][0:32, 0:512], t["ps0"][0:32, :], AF.Copy, sem="sSC")
    kb.act(t["u_bf"][0:32, 512:1024], t["ps1"][0:32, :], AF.Copy, sem="sSC")
    v_usc = kb.act(t["u_bf"][0:32, 1024:T], t["ps2"][0:32, 0:3], AF.Copy, sem="sSC")

    if first:
        kb.wait("tensor", "sSC", v_usc)
        for i in range(DT_ND):
            lhs = t["AT_bf"][:, 32 * i:32 * (i + 1)]
            v_pe = kb.mm(t["ps0"][0:32, 0:32], lhs, lhs, i == 0, i == DT_ND - 1,
                         sem="sPE" if i == DT_ND - 1 else None)
        kb.wait("scalar", "sPE", v_pe)
        v_sc = kb.act(t["G_bf"][0:32, :], t["ps0"][0:32, 0:32], AF.Copy, sem="sSC")
        kb.wait("tensor", "sSC", v_sc)
        v_gbt_sc = [0, 0]
        for et in range(ET_FFN):
            b = et % 2
            kb.wait("tensor", "sSC", v_gbt_sc[b])
            v_pe = kb.mm(t["ps%d" % b][:, 0:32], t["BT_bf"][0:32, 128 * et:128 * (et + 1)],
                         t["G_bf"][0:32, :], True, True, sem="sPE")
            kb.wait("scalar", "sPE", v_pe)
            v_gbt_sc[b] = kb.act(t["GBT_sb"][:, 32 * et:32 * (et + 1)], t["ps%d" % b][:, 0:32],
                                 AF.Copy, sem="sSC")
        kb.wait("tensor", "sSC", v_gbt_sc[0])
        kb.wait("tensor", "sSC", v_gbt_sc[1])

    v_wld = [0] * ET_FFN
    v_wc = [0] * ET_FFN
    v_xld = [0] * ET_FFN
    v_sq = [0] * ET_FFN
    v_comb = [0] * ET_FFN
    v_out = [0] * ET_FFN
    v_rowtmp = [0] * ET_FFN
    v_hs_val = [0] * ET_FFN
    v_watev = [0] * ET_FFN
    v_sqrt = [0] * ET_FFN
    v_stop = [0] * ET_FFN
    v_t1stop = [0] * ET_FFN
    rot_ev = {0: 0, 1: 0, 2: 0, 3: 0}
    rot_i = [0]
    CH_SPECS = ((0, 512), (512, 512), (1024, 3))

    def emit_wload(et):
        par = et % 2
        if et >= 2:
            kb.wait("gpsimd", "sPE", v_stop[et - 2])
            if first:
                kb.wait("gpsimd", "sDV", v_sq[et - 2])
                kb.wait("gpsimd", "sWC", v_wc[et - 2])
        ws = "sWB%d" % par
        if first:
            v_wld[et] = kb.dma("gpsimd", t["wbuf"][:, par],
                               t["ffn_wT"][:, 128 * et:128 * (et + 1)].rearrange("(t p) e -> p t e", p=128),
                               ws)
            kb.wait("gpsimd", ws, v_wld[et])
            v_wc[et] = kb.dma("gpsimd", t["scr_wb"][et], t["wbuf"][:, par].rearrange("p t e -> p (t e)"), "sWC")
        else:
            v_wld[et] = kb.dma("gpsimd", t["wbuf"][:, par],
                               t["scr_wb"][et].rearrange("p (t e) -> p t e", e=128), ws)

    def emit_xload(et):
        par = et % 2
        if et >= 2:
            kb.wait("scalar", "sDV", v_comb[et - 2])
        v_xld[et] = kb.dma("scalar", t["xres"][:, par, :],
                           (t["xT"] if first else t["scr_x"])[128 * et:128 * (et + 1), :], "sXR%d" % par)

    def emit_sq(et):
        par = et % 2
        kb.wait("vector", "sWB%d" % par, v_wld[et])
        if et >= 1:
            kb.wait("vector", "sPE", v_t1stop[et - 1])
        flat = t["wbuf"][:, par].rearrange("p t e -> p (t e)")
        v_sq[et] = kb.dve(lambda e: e.tensor_mul(out=t["sqb"][:], in0=flat, in1=flat), sem="sDV")

    def emit_passes(et):
        par = et % 2
        t1b = t["ps4"] if par == 0 else t["ps5"]
        kb.wait("tensor", "sWB%d" % par, v_wld[et])
        if first:
            kb.wait("tensor", "sDV", v_sq[et])
            if et >= 2:
                kb.wait("tensor", "sSC", v_rowtmp[et - 2])
            if et >= 1:
                kb.wait("tensor", "sSC", v_watev[et - 1])
        if et == 0:
            kb.wait("tensor", "sSC", v_usc)
        for ci, (off, w) in enumerate(CH_SPECS):
            rb = rot_i[0] % 4
            rot_i[0] += 1
            bank = t["ps%d" % rb]
            kb.wait("tensor", "sSC", rot_ev[rb])
            for dd in range(DT_ND):
                lhs = t["wbuf"][:, par, dd, :]
                st = dd == 0
                kb.mm(bank[:, 0:w], lhs, t["xn"][:, T * dd + off:T * dd + off + w], st, False)
                if first and ci == 0:
                    v_t1stop[et] = kb.mm(t1b[0:1, 0:128], t["ones_col_bf"][:],
                                         t["sqb"][:, 128 * dd:128 * dd + 128], st, dd == DT_ND - 1,
                                         sem="sPE" if dd == DT_ND - 1 else None)
                if first and ci == 1:
                    kb.mm(t["ps6"][:, 0:32], lhs, t["AT_bf"][:, 32 * dd:32 * (dd + 1)],
                          st, dd == DT_ND - 1)
            lhsB = t["BT_bf"][0:32, 128 * et:128 * (et + 1)]
            v = kb.mm(bank[:, 0:w], lhsB, t["u_bf"][0:32, off:off + w], False, True, sem="sPE")
            if ci == 2:
                v_stop[et] = v
            kb.wait("scalar", "sPE", v)
            rot_ev[rb] = kb.act(t["xpraw"][:, par, off:off + w], bank[:, 0:w], AF.Copy, sem="sSC")

    def emit_colnorm(et):
        par = et % 2
        t1b = t["ps4"] if par == 0 else t["ps5"]
        if et == 0:
            kb.wait("vector", "sWL", kb.state_preload_wl)
        kb.wait("scalar", "sPE", v_stop[et])
        v_watev[et] = kb.act(t["watmp2"][:, 0:32], t["ps6"][:, 0:32], AF.Copy, sem="sSC")
        kb.wait("vector", "sSC", v_watev[et])
        kb.dve(lambda e: e.tensor_mul(out=t["prodc"][:, 0:32], in0=t["watmp2"][:, 0:32],
                                      in1=t["BTT_sb"][:, 32 * et:32 * (et + 1)]))
        v0 = kb.dve(lambda e: e.tensor_mul(out=t["prodc"][:, 32:64], in0=t["GBT_sb"][:, 32 * et:32 * (et + 1)],
                                           in1=t["BTT_sb"][:, 32 * et:32 * (et + 1)]), sem="sDV")
        kb.wait("vector", "sDV", v0)
        kb.dve(lambda e: e.reduce_sum(out=t["coltmp"][:, 2:3], in_=t["prodc"][:, 0:32],
                                      axis=mybir.AxisListType.X))
        v_dv = kb.dve(lambda e: e.reduce_sum(out=t["coltmp"][:, 3:4], in_=t["prodc"][:, 32:64],
                                             axis=mybir.AxisListType.X), sem="sDV")
        kb.wait("vector", "sDV", v_dv)
        kb.wait("scalar", "sPE", v_t1stop[et])
        v_rowtmp[et] = kb.act(t["rowtmp"][0:1, 0:128], t1b[0:1, 0:128], AF.Copy, sem="sSC")
        kb.wait("tensor", "sSC", v_rowtmp[et])
        if et >= 1:
            kb.wait("tensor", "sSC", v_sqrt[et - 1])
        v_cm = kb.mm(t["ps7"][:, 0:1], t["rowtmp"][0:1, 0:128], t["ones_row_f"][0:1, 0:1],
                     True, True, sem="sPE")
        kb.wait("scalar", "sPE", v_cm)
        kb.wait("scalar", "sDV", v_dv)
        v_sqrt[et] = kb.act(t["coltmp"][:, 4:5], t["ps7"][:, 0:1], AF.Copy, sem="sSC")
        kb.wait("vector", "sSC", v_sqrt[et])
        if DEV_DEBUG_OUT:
            kb.dve(lambda e: e.tensor_copy(out=t["t1col_sb"][:, et:et + 1], in_=t["coltmp"][:, 4:5]))
            kb.dve(lambda e: e.tensor_copy(out=t["watcol_sb"][:, 2 * et % 64:2 * et % 64 + 1], in_=t["coltmp"][:, 2:3]))
            kb.dve(lambda e: e.tensor_copy(out=t["watcol_sb"][:, (2 * et + 1) % 64:(2 * et + 1) % 64 + 1], in_=t["coltmp"][:, 3:4]))
        v0 = kb.dve(lambda e: e.scalar_tensor_tensor(out=t["coltmp"][:, 5:6], in0=t["coltmp"][:, 2:3],
                                                     scalar=2.0, in1=t["coltmp"][:, 4:5],
                                                     op0=ALU.mult, op1=ALU.add), sem="sDV")
        kb.wait("vector", "sDV", v0)
        v0 = kb.dve(lambda e: e.tensor_add(out=t["coltmp"][:, 8:9], in0=t["coltmp"][:, 5:6],
                                           in1=t["coltmp"][:, 3:4]), sem="sDV")
        kb.wait("vector", "sDV", v0)
        v = kb.dve(lambda e: e.tensor_scalar_max(out=t["coltmp"][:, 9:10], in0=t["coltmp"][:, 8:9],
                                                 scalar1=1e-30), sem="sDV")
        kb.wait("scalar", "sDV", v)
        v = kb.act(t["coltmp"][:, 6:7], t["coltmp"][:, 9:10], AF.Sqrt, sem="sSC")
        kb.wait("vector", "sSC", v)
        v0 = kb.dve(lambda e: e.reciprocal(out=t["coltmp"][:, 7:8], in_=t["coltmp"][:, 6:7]), sem="sDV")
        kb.wait("vector", "sDV", v0)
        v_hs_val[et] = kb.dve(lambda e: e.scalar_tensor_tensor(
            out=t["halfscale"][:, et:et + 1], in0=t["coltmp"][:, 7:8], scalar=0.5,
            in1=t["consts"][:, CO_MAG + et:CO_MAG + et + 1], op0=ALU.mult, op1=ALU.mult), sem="sDV")
        kb.wait("vector", "sDV", v_hs_val[et])

    def emit_combine(et):
        par = et % 2
        kb.wait("vector", "sXR%d" % par, v_xld[et])
        if et >= 2:
            kb.wait("vector", "sXP%d" % par, v_out[et - 2])
        kb.wait("vector", "sSC", rot_ev[(rot_i[0] - 1) % 4])
        if first:
            kb.wait("vector", "sDV", v_hs_val[et])
        hs = t["halfscale"][:, et:et + 1]
        kb.dve(lambda e: e.scalar_tensor_tensor(out=t["xp"][:, par, 0:512], in0=t["xpraw"][:, par, 0:512],
                                                scalar=hs, in1=t["xres"][:, par, 0:512],
                                                op0=ALU.mult, op1=ALU.add), sem="sDV")
        kb.dve(lambda e: e.scalar_tensor_tensor(out=t["xp"][:, par, 512:1024], in0=t["xpraw"][:, par, 512:1024],
                                                scalar=hs, in1=t["xres"][:, par, 512:1024],
                                                op0=ALU.mult, op1=ALU.add), sem="sDV")
        v_comb[et] = kb.dve(lambda e: e.scalar_tensor_tensor(
            out=t["xp"][:, par, 1024:T], in0=t["xpraw"][:, par, 1024:T], scalar=hs,
            in1=t["xres"][:, par, 1024:T], op0=ALU.mult, op1=ALU.add), sem="sDV")
        kb.wait("gpsimd", "sDV", v_comb[et])
        if final:
            v_out[et] = kb.dma("gpsimd", t["out_fm"][128 * et:128 * (et + 1), :],
                               t["xp"][:, par, HALO:T], "sXP%d" % par)
        else:
            v_out[et] = kb.dma("gpsimd", t["scr_x"][128 * et:128 * (et + 1), :], t["xp"][:, par, :], "sXP%d" % par)

    for et in range(ET_FFN):
        emit_wload(et)
        emit_xload(et)
        if first:
            emit_sq(et)
        emit_passes(et)
        if first:
            emit_colnorm(et)
        emit_combine(et)
    kb.state_ffn_xp0 = kb.val("sXP0")
    kb.state_ffn_xp1 = kb.val("sXP1")
    kb.state_ffn_pe = kb.val("sPE")


PROJ_ORDER = list(range(64, 144)) + [144] + list(range(0, 64))


def _tile_kind(et):
    if et < 64:
        return "gate"
    if et < 128:
        return "hs"
    if et < 136:
        return "B"
    if et < 144:
        return "C"
    return "dt"


def _sp_chain_act(kb, out, in_, func, bias=0.0, scale=1.0):
    v = kb.act(out, in_, func, bias=bias, scale=scale, sem="sSC")
    kb.wait("scalar", "sSC", v)
    return v


def _sp_chain_dve(kb, fn):
    v = kb.dve(fn, sem="sDV")
    kb.wait("vector", "sDV", v)
    return v


def emit_inproj(kb: KB):
    t = kb.t
    CH_SPECS = ((0, 512), (512, 512), (1024, 3))
    rot_ev = {0: 0, 1: 0, 2: 0, 3: 0}
    rot_i = [0]
    tr_ev = {4: 0, 5: 0}
    nET = len(PROJ_ORDER)
    v_wld = [0] * nET
    v_stop = [0] * nET
    v_consume = [0] * nET
    v_spill = {}
    v_convprev = [0]
    v_fmspill = {}
    kb.wait("tensor", "sDV", kb.state_xn_dv_base + 2 * DT_ND)
    kb.wait("tensor", "sWL", kb.state_preload_wl)
    kb.wait("vector", "sWL", kb.state_preload_wl)

    for idx, et in enumerate(PROJ_ORDER):
        par = idx % 2
        kind = _tile_kind(et)
        if idx >= 2:
            kb.wait("gpsimd", "sPE", v_stop[idx - 2])
        v_wld[idx] = kb.dma("gpsimd", t["wbuf"][:, par],
                            t["inproj_wT"][:, 128 * et:128 * (et + 1)].rearrange("(t p) e -> p t e", p=128),
                            "sWB%d" % par)
        kb.wait("tensor", "sWB%d" % par, v_wld[idx])
        banks = []
        v = 0
        for ci, (off, w) in enumerate(CH_SPECS):
            rb = rot_i[0] % 4
            rot_i[0] += 1
            banks.append(rb)
            kb.wait("tensor", "sSC", rot_ev[rb])
            for dd in range(DT_ND):
                lhs = t["wbuf"][:, par, dd, :]
                v = kb.mm(t["ps%d" % rb][:, 0:w], lhs, t["xn"][:, T * dd + off:T * dd + off + w],
                          dd == 0, dd == DT_ND - 1,
                          sem="sPE" if (ci == 2 and dd == DT_ND - 1) else None)
        v_stop[idx] = v
        b0, b1, b2 = banks

        if kind == "gate":
            kb.wait("scalar", "sPE", v_stop[idx])
            if idx >= 2:
                kb.wait("scalar", "sPE", v_consume[idx - 2])
                s2, vv2 = v_spill.get(idx - 2, (None, 0))
                if s2:
                    kb.wait("scalar", s2, vv2)
            rot_ev[b0] = kb.act(t["gbuf"][:, par, 0:509], t["ps%d" % b0][:, 3:512], AF.Silu, sem="sSC")
            rot_ev[b1] = kb.act(t["gbuf"][:, par, 509:1021], t["ps%d" % b1][:, 0:512], AF.Silu, sem="sSC")
            rot_ev[b2] = v_gev = kb.act(t["gbuf"][:, par, 1021:1024], t["ps%d" % b2][:, 0:3], AF.Silu, sem="sSC")
        elif kind in ("hs", "B", "C"):
            ti = et - 64
            kb.wait("scalar", "sPE", v_stop[idx])
            if idx >= 2:
                kb.wait("scalar", "sPE", v_consume[idx - 2])
                s2, vv2 = v_spill.get(idx - 2, (None, 0))
                if s2:
                    kb.wait("scalar", s2, vv2)
            rot_ev[b0] = kb.act(t["hbcp"][:, par, 0:512], t["ps%d" % b0][:, 0:512], AF.Copy, sem="sSC")
            rot_ev[b1] = kb.act(t["hbcp"][:, par, 512:1024], t["ps%d" % b1][:, 0:512], AF.Copy, sem="sSC")
            rot_ev[b2] = v_hbcp = kb.act(t["hbcp"][:, par, 1024:T], t["ps%d" % b2][:, 0:3], AF.Copy, sem="sSC")
            kb.wait("vector", "sPE", v_convprev[0])
            for k in range(KCONV):
                vd = kb.dve(lambda e, k=k, ti=ti: e.tensor_scalar_mul(
                    out=t["diag"][:, k, :], in0=t["ident_bf"][:],
                    scalar1=t["convw_sb"][:, k, ti:ti + 1]), sem="sDV")
            cb1 = rot_i[0] % 4
            rot_i[0] += 1
            cb2 = rot_i[0] % 4
            rot_i[0] += 1
            kb.wait("tensor", "sDV", vd)
            kb.wait("tensor", "sSC", v_hbcp)
            kb.wait("tensor", "sSC", rot_ev[cb1])
            kb.wait("tensor", "sSC", rot_ev[cb2])
            for k in range(KCONV):
                kb.mm(t["ps%d" % cb1][:, 0:512], t["diag"][:, k, :], t["hbcp"][:, par, k:k + 512],
                      k == 0, k == KCONV - 1)
                v_cv = kb.mm(t["ps%d" % cb2][:, 0:512], t["diag"][:, k, :], t["hbcp"][:, par, 512 + k:1024 + k],
                             k == 0, k == KCONV - 1, sem="sPE" if k == KCONV - 1 else None)
            v_convprev[0] = v_cv
            kb.wait("scalar", "sPE", v_cv)
            bias_ap = t["consts"][:, CO_CONVB + ti:CO_CONVB + ti + 1]
            rot_ev[cb1] = kb.act(t["gbuf"][:, par, 0:512], t["ps%d" % cb1][:, 0:512], AF.Silu, bias=bias_ap, sem="sSC")
            rot_ev[cb2] = v_gev = kb.act(t["gbuf"][:, par, 512:1024], t["ps%d" % cb2][:, 0:512], AF.Silu,
                                         bias=bias_ap, sem="sSC")
        else:
            kb.wait("scalar", "sPE", v_stop[idx])
            rot_ev[b0] = kb.act(t["dtt"][:, 0:512], t["ps%d" % b0][:, 0:512], AF.Copy, sem="sSC")
            rot_ev[b1] = kb.act(t["dtt"][:, 512:1024], t["ps%d" % b1][:, 0:512], AF.Copy, sem="sSC")
            rot_ev[b2] = vdt = kb.act(t["dtt"][:, 1024:T], t["ps%d" % b2][:, 0:3], AF.Copy, sem="sSC")
            emit_dt_chain(kb, vdt)
            v_consume[idx] = v_stop[idx]
            continue

        if kind == "C":
            kb.wait("gpsimd", "sSC", v_gev)
            gi = et - 136
            vsp = kb.dma("gpsimd", t["scr_C_fm"][128 * gi:128 * (gi + 1), :], t["gbuf"][:, par, :],
                         "sSP%d" % par)
            v_fmspill[idx] = ("sSP%d" % par, vsp)
            v_spill[idx] = ("sSP%d" % par, vsp)
            v_consume[idx] = v_stop[idx]
        else:
            kb.wait("tensor", "sSC", v_gev)
            s2, vv2 = v_spill.get(idx - 2, (None, 0))
            if s2:
                kb.wait("scalar", s2, vv2)
            vtr = 0
            for z in range(NZ):
                tb = 4 + (z % 2)
                kb.wait("tensor", "sSC", tr_ev[tb])
                vt = kb.trans(t["ps%d" % tb][:, 0:256].bitcast(BF16)[:, 0:128],
                              t["gbuf"][:, par, 128 * z:128 * (z + 1)], t["ident_bf"][:], sem="sPE")
                kb.wait("scalar", "sPE", vt)
                tr_ev[tb] = vtr = kb.act(t["tstg"][:, par, z, :], t["ps%d" % tb][:, 0:256].bitcast(BF16)[:, 0:128],
                                         AF.Copy, sem="sSC")
            v_consume[idx] = vt
            kb.wait("gpsimd", "sSC", vtr)
            if kind == "hs":
                dstv = t["scr_hs_tm"].rearrange("(z l) (tt e) -> z l tt e", l=128, e=128)[:, :, et - 64, :]
            elif kind == "gate":
                dstv = t["scr_gate_tm"].rearrange("(z l) (tt e) -> z l tt e", l=128, e=128)[:, :, et, :]
            else:
                dstv = t["scr_B_tm"].rearrange("(z l) (g n) -> z l g n", l=128, n=128)[:, :, et - 128, :]
            vsp = kb.dma("gpsimd", dstv.transpose([1, 0, 2]), t["tstg"][:, par], "sSP%d" % par)
            if kind == "B":
                gi = et - 128
                kb.wait("gpsimd", "sSC", v_gev)
                vsp = kb.dma("gpsimd", t["scr_B_fm"][128 * gi:128 * (gi + 1), :], t["gbuf"][:, par, :],
                             "sSP%d" % par)
            v_spill[idx] = ("sSP%d" % par, vsp)
    kb.state_inproj_pe = kb.val("sPE")
    kb.state_inproj_sp = (kb.val("sSP0"), kb.val("sSP1"))


def emit_dt_chain(kb: KB, vdt):
    t = kb.t
    sp0 = t["spbuf"][:, 0, 0:1024]
    sp1 = t["spbuf"][:, 1, 0:1024]
    c2 = t["dtt"][:, 0:1024]
    kb.wait("vector", "sSC", vdt)
    v = kb.dve(lambda e: e.tensor_scalar_add(out=sp0, in0=t["dtt"][:, 3:T],
                                             scalar1=t["consts"][:, CO_DTB:CO_DTB + 1]), sem="sDV")
    kb.wait("vector", "sDV", v)
    kb.wait("scalar", "sDV", v)
    v = kb.act(sp1, sp0, AF.Abs, sem="sSC")
    kb.wait("scalar", "sSC", v)
    v = kb.act(c2, sp1, AF.Exp, scale=-1.0, sem="sSC")
    kb.wait("scalar", "sSC", v)
    kb.wait("scalar", "sDV", kb.state_preload_dv)
    v_ln = kb.act(sp1, c2, AF.Ln, bias=t["consts"][:, CO_ONE:CO_ONE + 1], sem="sSC")
    kb.wait("vector", "sSC", v_ln)
    v = kb.dve(lambda e: e.tensor_scalar_max(out=t["dtfm"][:, :], in0=sp0, scalar1=0.0), sem="sDV")
    kb.wait("vector", "sDV", v)
    v = kb.dve(lambda e: e.tensor_add(out=t["dtfm"][:, :], in0=t["dtfm"][:, :], in1=sp1), sem="sDV")
    kb.wait("vector", "sDV", v)
    v = kb.dve(lambda e: e.tensor_scalar_min(out=t["dtfm"][:, :], in0=t["dtfm"][:, :], scalar1=DT_MAX), sem="sDV")
    kb.wait("vector", "sDV", v)
    v = kb.dve(lambda e: e.tensor_scalar_max(out=sp0, in0=t["dtfm"][:, :], scalar1=1e-38), sem="sDV")
    kb.wait("vector", "sDV", v)
    kb.wait("scalar", "sDV", v)
    v = kb.act(t["lndt"][:, :], sp0, AF.Ln, sem="sSC")
    kb.wait("scalar", "sSC", v)
    kb.state_lndt_sc = v
    v = kb.act(t["coltmp"][:, 8:9], t["consts"][:, CO_ALOG:CO_ALOG + 1], AF.Exp, sem="sSC")
    kb.wait("vector", "sSC", v)
    v = kb.dve(lambda e: e.tensor_scalar_mul(out=t["coltmp"][:, 9:10], in0=t["coltmp"][:, 8:9],
                                             scalar1=-1.0), sem="sDV")
    kb.wait("vector", "sDV", v)
    v = kb.dve(lambda e: e.tensor_scalar_mul(out=t["adt"][:, :], in0=t["dtfm"][:, :],
                                             scalar1=t["coltmp"][:, 9:10]), sem="sDV")
    kb.wait("vector", "sDV", v)
    kb.state_adt_dv = v


def emit_chunk_prep(kb: KB):
    t = kb.t
    kb.wait("tensor", "sDV", kb.state_adt_dv)
    v_ev = {6: 0, 7: 0}
    for z in range(NZ):
        kb.wait("tensor", "sSC", v_ev[6])
        vt = kb.inc(None, "sPE", 1)
        if kb.on("tensor"):
            kb.eng.transpose(out=t["ps6"][:, 0:128], in_=t["adt"][:, 128 * z:128 * (z + 1)],
                             identity=t["ident_f"][:]).then_inc(kb.sem["sPE"], 1)
        kb.wait("scalar", "sPE", vt)
        v_ev[6] = v1 = kb.act(t["ctm1"][:, :], t["ps6"][:, 0:128], AF.Copy, sem="sSC")
        kb.wait("tensor", "sSC", v1)
        kb.wait("tensor", "sSC", v_ev[7])
        v2 = kb.mm(t["ps7"][:, 0:128], t["cumU_f"][:], t["ctm1"][:, :], True, True, sem="sPE")
        kb.wait("scalar", "sPE", v2)
        v_ev[7] = v3 = kb.act(t["actm"][:, 128 * z:128 * (z + 1)], t["ps7"][:, 0:128], AF.Copy, sem="sSC")
        kb.wait("tensor", "sSC", v3)
        kb.wait("tensor", "sSC", v_ev[6])
        vt2 = kb.inc(None, "sPE", 1)
        if kb.on("tensor"):
            kb.eng.transpose(out=t["ps6"][:, 0:128], in_=t["actm"][:, 128 * z:128 * (z + 1)],
                             identity=t["ident_f"][:]).then_inc(kb.sem["sPE"], 1)
        kb.wait("scalar", "sPE", vt2)
        v_ev[6] = kb.act(t["acfm"][:, 128 * z:128 * (z + 1)], t["ps6"][:, 0:128], AF.Copy, sem="sSC")
    v_last = v_ev[6]
    kb.wait("vector", "sSC", v_last)
    _sp_chain_dve(kb, lambda e: e.tensor_copy(out=t["dscol"][:, :], in_=t["acfm"][:, 127:128]))
    for z in range(1, NZ):
        _sp_chain_dve(kb, lambda e, z=z: e.tensor_add(out=t["dscol"][:, :], in0=t["dscol"][:, :],
                                                      in1=t["acfm"][:, 128 * z + 127:128 * z + 128]))
    kb.state_dseg_dv = kb.val("sDV")
    kb.wait("gpsimd", "sSC", v_last)
    kb.wait("gpsimd", "sSC", kb.state_lndt_sc)
    kb.dma("gpsimd", t["scr_acum"].rearrange("z h l -> h z l"),
           t["acfm"][:, :].rearrange("h (z l) -> h z l", l=128), "sSP0")
    kb.dma("gpsimd", t["scr_actm"].rearrange("z s h -> s z h"),
           t["actm"][:, :].rearrange("s (z h) -> s z h", h=128), "sSP0")
    kb.dma("gpsimd", t["scr_dt"][:, 0:1024], t["dtfm"][:, :], "sSP0")
    kb.dma("gpsimd", t["scr_dt"][:, 1024:2048], t["lndt"][:, :], "sSP0")
    kb.dma("gpsimd", t["scr_hscale"][:], t["halfscale"][:], "sSP0")
    kb.state_prep_sp = kb.val("sSP0")


def emit_states(kb: KB):
    t = kb.t
    kb.wait("vector", "sPE", kb.state_inproj_pe)
    kb.wait("gpsimd", "sPE", kb.state_inproj_pe)
    v_memset = _sp_chain_dve(kb, lambda e: e.memset(t["prev"][:, :], 0.0))
    v_hld = [0] * NZ
    v_bld = [0] * NZ
    v_used = [0] * NZ
    v_spl = [0] * NZ
    v_evz = [0] * NZ
    v_mb_pe = 0
    v_ps6ev = [0]
    rot = {0: 0, 1: 0, 2: 0, 3: 0}
    ri = [0]
    for z in range(NZ):
        par = z % 2
        if z >= 2:
            kb.wait("gpsimd", "sPE", v_used[z - 2])
        v_hld[z] = kb.dma("gpsimd", t["hstch"][:, par, :],
                          t["scr_hs_tm"][128 * z:128 * (z + 1), :], "sHS%d" % par)
        v_bld[z] = kb.dma("gpsimd", t["btmch"][:, par, :],
                          t["scr_B_tm"][128 * z:128 * (z + 1), :], "sBT%d" % par)
        aend = t["acfm"][:, 128 * z + 127:128 * z + 128]
        vx = _sp_chain_act(kb, t["ctm2"][:, :], t["acfm"][:, 128 * z:128 * (z + 1)], AF.Exp,
                           scale=-1.0, bias=aend)
        kb.wait("vector", "sSC", vx)
        vds = _sp_chain_dve(kb, lambda e, z=z: e.tensor_mul(out=t["ctm2"][:, :], in0=t["ctm2"][:, :],
                                                            in1=t["dtfm"][:, 128 * z:128 * (z + 1)]))
        kb.wait("tensor", "sDV", vds)
        kb.wait("tensor", "sSC", v_ps6ev[0])
        vt = kb.trans(t["ps6"][:, 0:128], t["ctm2"][:, :], t["ident_f"][:], sem="sPE")
        kb.wait("scalar", "sPE", vt)
        v_ps6ev[0] = v3 = kb.act(t["ctm3"][:, :], t["ps6"][:, 0:128], AF.Copy, sem="sSC")
        kb.wait("vector", "sSC", v3)
        kb.wait("vector", "sHS%d" % par, v_hld[z])
        if z >= 1:
            kb.wait("vector", "sPE", v_used[z - 1])
        for h in range(H):
            vq = kb.dve(lambda e, h=h, par=par: e.tensor_scalar_mul(
                out=t["hsc"][:, 64 * h:64 * (h + 1)], in0=t["hstch"][:, par, 64 * h:64 * (h + 1)],
                scalar1=t["ctm3"][:, h:h + 1]), sem="sDV")
        kb.wait("tensor", "sDV", vq)
        kb.wait("tensor", "sBT%d" % par, v_bld[z])
        v_mm = 0
        vstz = 0
        for g in range(G):
            for half in range(2):
                rb = ri[0] % 4
                ri[0] += 1
                kb.wait("tensor", "sSC", rot[rb])
                for j in range(8):
                    h = 16 * g + 8 * half + j
                    v_mm = kb.mm(t["ps%d" % rb][:, 64 * (h % 8):64 * (h % 8) + 64],
                                 t["btmch"][:, par, 128 * g:128 * (g + 1)],
                                 t["hsc"][:, 64 * h:64 * (h + 1)], j == 0, j == 7,
                                 sem="sPE")
                base = 1024 * g + 512 * half
                kb.wait("scalar", "sPE", v_mm)
                rot[rb] = vstz = kb.act(t["stz"][:, par, base:base + 512], t["ps%d" % rb][:, 0:512],
                                        AF.Copy, sem="sSC")
        v_used[z] = v_mm
        kb.wait("gpsimd", "sSC", vstz)
        if z >= 2:
            kb.wait("gpsimd", "sSZ%d" % par, v_spl[z - 2])
        v_spl[z] = kb.dma("gpsimd", t["scr_states"][z], t["stz"][:, par, :], "sSZ%d" % par)
        vec = _sp_chain_act(kb, t["ecol"][:, :], aend, AF.Exp)
        kb.wait("gpsimd", "sSC", vec)
        if z >= 1:
            kb.wait("gpsimd", "sPE", v_mb_pe)
        vw = kb.dma("gpsimd", t["scr_row"][2:3, 0:128], t["ecol"][:, :], "sMB")
        kb.wait("gpsimd", "sMB", vw)
        vr = kb.dma("gpsimd", t["erow"], t["scr_row"][2:3, 0:128], "sMB")
        kb.wait("tensor", "sMB", vr)
        kb.wait("tensor", "sSC", v_ps6ev[0])
        v_mb_pe = kb.mm(t["ps6"][:, 0:128], t["ones_row_f"][0:1, 0:128], t["erow"],
                        True, True, sem="sPE")
        kb.wait("scalar", "sPE", v_mb_pe)
        v_ps6ev[0] = vesb = kb.act(t["esb"][:, :], t["ps6"][:, 0:128], AF.Copy, sem="sSC")
        kb.wait("vector", "sSC", vesb)
        kb.wait("vector", "sSC", vstz)
        if z == 0:
            kb.wait("vector", "sDV", v_memset)
        for h in range(H):
            vu = kb.dve(lambda e, h=h, par=par: e.scalar_tensor_tensor(
                out=t["prev"][:, 64 * h:64 * (h + 1)], in0=t["prev"][:, 64 * h:64 * (h + 1)],
                scalar=t["esb"][:, h:h + 1], in1=t["stz"][:, par, 64 * h:64 * (h + 1)],
                op0=ALU.mult, op1=ALU.add), sem="sDV")
        kb.wait("vector", "sDV", vu)
    kb.wait("scalar", "sDV", vu)
    kb.wait("scalar", "sSZ0", v_spl[6])
    vb = _sp_chain_act(kb, t["stz"][:, 0, :], t["prev"][:, :], AF.Copy)
    kb.wait("gpsimd", "sSC", vb)
    kb.wait("gpsimd", "sDV", kb.state_dseg_dv)
    kb.dma("gpsimd", t["out_state"][:], t["stz"][:, 0, :], "sSO")
    v = kb.dma("gpsimd", t["out_dseg"][:], t["dscol"][:, :], "sSO")
    kb.state_states_st = v

def _ebcast(kb, acol, v_gate_pe):
    t = kb.t
    kb.wait("gpsimd", "sPE", v_gate_pe[0])
    vw = kb.dma("gpsimd", t["scr_row"][2:3, 0:128], acol, "sMB")
    kb.wait("gpsimd", "sMB", vw)
    vr = kb.dma("gpsimd", t["erow"], t["scr_row"][2:3, 0:128], "sMB")
    kb.wait("tensor", "sMB", vr)
    kb.wait("tensor", "sSC", v_gate_pe[1])
    v_pe = kb.mm(t["ps6"][:, 0:128], t["ones_row_f"][0:1, 0:128], t["erow"], True, True, sem="sPE")
    v_gate_pe[0] = v_pe
    kb.wait("scalar", "sPE", v_pe)
    v_gate_pe[1] = v = kb.act(t["esb"][:, :], t["ps6"][:, 0:128], AF.Copy, sem="sSC")
    return v


def emit_state_combine(kb: KB):
    t = kb.t
    kb.dma("gpsimd", t["dsegsb"][:, :], t["dseg_all"][:], "sPL0")
    v_sm = kb.dma("gpsimd", t["smaskb"][:, :], t["segmask"][:].partition_broadcast(128), "sPL0")
    kb.wait("vector", "sPL0", v_sm)
    v_memset = _sp_chain_dve(kb, lambda e: e.memset(t["prev"][:, :], 0.0))
    vmd = _sp_chain_dve(kb, lambda e: e.tensor_mul(out=t["msml"][:, :], in0=t["dsegsb"][:, :],
                                                   in1=t["smaskb"][:, :]))
    ebgate = [0, 0]
    v_ld = 0
    vu = v_memset
    for j in range(NCORES):
        kb.wait("scalar", "sDV", vmd)
        vex = _sp_chain_act(kb, t["ecol"][:, :], t["msml"][:, j:j + 1], AF.Exp)
        kb.wait("gpsimd", "sSC", vex)
        kb.wait("gpsimd", "sDV", vu)
        v_ld = kb.dma("gpsimd", t["stzld"][:], t["state_all"][j], "sSZ0")
        vesb = _ebcast(kb, t["ecol"][:, :], ebgate)
        kb.wait("vector", "sSZ0", v_ld)
        vms = _sp_chain_dve(kb, lambda e, j=j: e.tensor_scalar_mul(
            out=t["stzld"][:], in0=t["stzld"][:], scalar1=t["smaskb"][:, j:j + 1]))
        kb.wait("vector", "sSC", vesb)
        for h in range(H):
            vu = kb.dve(lambda e, h=h: e.scalar_tensor_tensor(
                out=t["prev"][:, 64 * h:64 * (h + 1)], in0=t["prev"][:, 64 * h:64 * (h + 1)],
                scalar=t["esb"][:, h:h + 1], in1=t["stzld"][:, 64 * h:64 * (h + 1)],
                op0=ALU.mult, op1=ALU.add), sem="sDV")
        kb.wait("vector", "sDV", vu)
    kb.state_comb_dv = vu
    kb.state_comb_eb = ebgate


def emit_scan(kb: KB):
    t = kb.t
    kb.dma("gpsimd", t["dtfm"][:, :], t["scr_dt"][:, 0:1024], "sPL0")
    kb.dma("gpsimd", t["lndt"][:, :], t["scr_dt"][:, 1024:2048], "sPL0")
    kb.dma("gpsimd", t["acfm"][:, :].rearrange("h (z l) -> h z l", l=128),
           t["scr_acum"].rearrange("z h l -> h z l"), "sPL0")
    kb.dma("gpsimd", t["actm"][:, :].rearrange("s (z h) -> s z h", h=128),
           t["scr_actm"].rearrange("z s h -> s z h"), "sPL0")
    v0 = kb.dma("gpsimd", t["halfscale"][:], t["scr_hscale"][:], "sPL0")
    kb.wait("tensor", "sWL", kb.state_preload_wl)
    kb.wait("tensor", "sSC", kb.state_comb_eb[1])
    vpe = kb.mm(t["ps6"][:, 0:64], t["dcolmm"][0:1, 0:128], t["ones_row_f"][0:1, 0:64],
                True, True, sem="sPE")
    kb.wait("scalar", "sPE", vpe)
    vsc = kb.act(t["dexps"][:], t["ps6"][:, 0:64], AF.Copy, sem="sSC")
    kb.wait("gpsimd", "sSC", vsc)
    vdx = kb.dma("gpsimd", t["scr_dexp"][:], t["dexps"], "sMB")
    kb.wait("gpsimd", "sMB", vdx)
    kb.wait("sync", "sMB", vdx)
    vdb = kb.dma("sync", t["dbc"][:],
                 t["scr_dexp"].rearrange("h p -> (h p)").unsqueeze(0).partition_broadcast(128), "sWL")
    if DEV_DEBUG_OUT:
        kb.wait("gpsimd", "sWL", vdb)
        vdbg = kb.dma("gpsimd", t["dbg_dbc"][:], t["dbc"][:], "sYS1")
        kb.wait("gpsimd", "sYS1", vdbg)
    kb.state_scan_pe = [kb.val("sPE")]

    rotg = {0: 0, 1: 0}
    v_g1_dv = {0: 0, 1: 0}
    yd_rot = {2: 0, 3: 0}
    yo_rot = {4: 0, 5: 0}
    tr_ev = {6: 0, 7: 0}
    v_prevu = kb.state_comb_dv
    v_hld = v_gld = v_sld = v_bcl = 0
    v_ystg_sp = 0
    v_d1 = [0, 0]
    v_d1_used = [0, 0]
    v_lexp = {0: 0, 1: 0}
    v_ltused = 0
    v_yn_prev = 0
    for z in range(NZ):
        kb.wait("gpsimd", "sPE", kb.state_scan_pe[0])
        kb.wait("gpsimd", "sDV", v_prevu)
        kb.wait("gpsimd", "sDV", v_yn_prev)
        v_hld = kb.dma("gpsimd", t["hstch1"][:], t["scr_hs_tm"][128 * z:128 * (z + 1), :], "sHS0")
        v_gld = kb.dma("gpsimd", t["gtch1"][:], t["scr_gate_tm"][128 * z:128 * (z + 1), :], "sGT0")
        v_sld = kb.dma("gpsimd", t["stzld"][:], t["scr_states"][z], "sSZ1")
        v_bcl = kb.dma("gpsimd", t["bfm"][:].rearrange("p g l -> p g l"),
                       t["scr_B_fm"][:, 128 * z:128 * (z + 1)].rearrange("(g p) l -> p g l", p=128), "sBT0")
        v_bcl = kb.dma("gpsimd", t["cfm"][:].rearrange("p g l -> p g l"),
                       t["scr_C_fm"][:, 128 * z:128 * (z + 1)].rearrange("(g p) l -> p g l", p=128), "sBT0")
        kb.wait("scalar", "sDV", v_prevu)
        kb.wait("scalar", "sYS0", v_ystg_sp)
        v_prevb = _sp_chain_act(kb, t["prevb"][:], t["prev"][:, :], AF.Copy)
        if z == 0:
            kb.wait("tensor", "sPL0", v0)
        kb.wait("tensor", "sSC", tr_ev[7])
        vt = kb.trans(t["ps7"][:, 0:128], t["lndt"][:, 128 * z:128 * (z + 1)], t["ident_f"][:], sem="sPE")
        kb.wait("scalar", "sPE", vt)
        tr_ev[7] = vsc = kb.act(t["lndt_tm"][:, :], t["ps7"][:, 0:128], AF.Copy, sem="sSC")
        kb.wait("vector", "sSC", vsc)
        vpb = _sp_chain_dve(kb, lambda e, z=z: e.tensor_sub(out=t["pbias_tm"][:, :], in0=t["lndt_tm"][:, :],
                                                            in1=t["actm"][:, 128 * z:128 * (z + 1)]))
        vex = _sp_chain_act(kb, t["expa"][:, :], t["actm"][:, 128 * z:128 * (z + 1)], AF.Exp)
        kb.wait("tensor", "sHS0", v_hld)
        kb.wait("tensor", "sBT0", v_bcl)
        kb.wait("vector", "sHS0", v_hld)
        kb.wait("vector", "sGT0", v_gld)
        if DEV_DEBUG_OUT and z == 1:
            kb.wait("gpsimd", "sBT0", v_bcl)
            kb.wait("gpsimd", "sHS0", v_hld)
            kb.dma("gpsimd", t["dbg_bc"][:, 0:1024], t["bfm"][:].rearrange("p g l -> p (g l)"), "sYS1")
            kb.dma("gpsimd", t["dbg_bc"][:, 1024:2048], t["cfm"][:].rearrange("p g l -> p (g l)"), "sYS1")
            vdbg = kb.dma("gpsimd", t["dbg_hst"][:], t["hstch1"][:], "sYS1")
            kb.wait("gpsimd", "sYS1", vdbg)
        v_m1 = 0
        kb.wait("tensor", "sSC", v_prevb)
        for g in range(G):
            for qi in range(4):
                pq = qi % 2
                h0 = 16 * g + 4 * qi
                kb.wait("gpsimd", "sDV", v_d1_used[pq])
                if g == 0 and qi == 0:
                    kb.wait("gpsimd", "sDV", v_yn_prev)
                v_d1[pq] = kb.dma("gpsimd", t["d1stg"][:, pq],
                                  t["scr_acum"][z].rearrange("h l -> (h l)").unsqueeze(0)
                                  [0:1, 128 * h0:128 * h0 + 512].partition_broadcast(128), "sD1%d" % pq)
                kb.wait("vector", "sD1%d" % pq, v_d1[pq])
                kb.wait("vector", "sSC", v_lexp[pq])
                if g == 0 and qi == 0:
                    kb.wait("vector", "sSC", vex)
                    kb.wait("vector", "sWL", kb.state_preload_wl)
                for hq in range(4):
                    h = h0 + hq
                    vst = kb.dve(lambda e, pq=pq, hq=hq, h=h: e.scalar_tensor_tensor(
                        out=t["lstage"][:, pq, 128 * hq:128 * (hq + 1)],
                        in0=t["d1stg"][:, pq, 128 * hq:128 * (hq + 1)],
                        scalar=t["pbias_tm"][:, h:h + 1], in1=t["negmask_f"][:],
                        op0=ALU.add, op1=ALU.add), sem="sDV")
                v_d1_used[pq] = vst
                kb.wait("scalar", "sDV", vst)
                if DEV_DEBUG_OUT and z == 1 and g == 3 and qi == 1:
                    kb.wait("gpsimd", "sDV", vst)
                    kb.dma("gpsimd", t["dbg_sc"][:, 0:512], t["d1stg"][:, 1], "sYS1")
                    kb.dma("gpsimd", t["dbg_sc"][:, 512:640], t["pbias_tm"][:, :], "sYS1")
                    kb.dma("gpsimd", t["dbg_sc"][:, 640:768], t["lndt_tm"][:, :], "sYS1")
                    vdbg = kb.dma("gpsimd", t["dbg_sc"][:, 768:896], t["expa"][:, :], "sYS1")
                    kb.wait("gpsimd", "sYS1", vdbg)
                kb.wait("scalar", "sDV", v_ltused)
                kb.act(t["lt16"][:, 4 * qi:4 * qi + 4, :].rearrange("p a b -> p (a b)"),
                       t["lstage"][:, pq, :], AF.Exp, sem="sSC")
                v_lexp[pq] = kb.val("sSC")
            v_lt = kb.val("sSC")
            if DEV_DEBUG_OUT and z == 1 and g == 3:
                kb.wait("gpsimd", "sSC", v_lt)
                vdbg = kb.dma("gpsimd", t["dbg_lt"][:],
                              t["lt16"][:].rearrange("p a b -> p (a b)"), "sYS1")
                kb.wait("gpsimd", "sYS1", vdbg)
            gb = g % 2
            kb.wait("tensor", "sDV", v_g1_dv[gb])
            v_g1 = kb.mm(t["ps%d" % gb][:, 0:128], t["bfm"][:, g, :], t["cfm"][:, g, :], True, True, sem="sPE")
            ydb = 2 + (g % 2)
            yob = 4 + (g % 2)
            kb.wait("tensor", "sDV", yd_rot[ydb])
            kb.wait("tensor", "sDV", yo_rot[yob])
            for j in range(16):
                h = 16 * g + j
                mpar = j % 2
                kb.wait("vector", "sPE", v_g1)
                kb.wait("vector", "sSC", v_lt)
                if j >= 2:
                    kb.wait("vector", "sPE", v_m1)
                vm = kb.dve(lambda e, j=j, gb=gb, mpar=mpar: e.tensor_mul(
                    out=t["m1buf"][:, mpar, :], in0=t["ps%d" % gb][:, 0:128],
                    in1=t["lt16"][:, j, :]), sem="sDV")
                kb.wait("tensor", "sDV", vm)
                if j == 8:
                    kb.wait("tensor", "sDV", yd_rot[ydb])
                kb.mm(t["ps%d" % ydb][:, 64 * (j % 8):64 * (j % 8) + 64], t["m1buf"][:, mpar, :],
                      t["hstch1"][:, 64 * h:64 * (h + 1)], j % 8 == 0, j % 8 == 7)
                v_m1 = kb.mm(t["ps%d" % yob][:, 64 * (j % 8):64 * (j % 8) + 64], t["cfm"][:, g, :],
                             t["prevb"][:, 64 * h:64 * (h + 1)], j % 8 == 0, j % 8 == 7, sem="sPE")
                if j == 7 or j == 15:
                    half = j // 8
                    base = 1024 * g + 512 * half
                    kb.wait("vector", "sPE", v_m1)
                    if g == 0 and half == 0:
                        kb.wait("vector", "sSC", v_prevb)
                    vt1 = kb.dve(lambda e, base=base: e.tensor_mul(
                        out=t["t1buf"][:, :], in0=t["dbc"][:, base:base + 512],
                        in1=t["hstch1"][:, base:base + 512]), sem="sDV")
                    kb.wait("vector", "sDV", vt1)
                    vya = kb.dve(lambda e, ydb=ydb: e.tensor_add(
                        out=t["t1buf"][:, :], in0=t["t1buf"][:, :], in1=t["ps%d" % ydb][:, 0:512]), sem="sDV")
                    kb.wait("vector", "sDV", vya)
                    for jj in range(8):
                        hh = 16 * g + 8 * half + jj
                        vyb = kb.dve(lambda e, hh=hh, jj=jj, yob=yob: e.scalar_tensor_tensor(
                            out=t["y_z"][:, 64 * hh:64 * (hh + 1)], in0=t["ps%d" % yob][:, 64 * jj:64 * jj + 64],
                            scalar=t["expa"][:, hh:hh + 1], in1=t["t1buf"][:, 64 * jj:64 * jj + 64],
                            op0=ALU.mult, op1=ALU.add), sem="sDV")
                    yd_rot[ydb] = yo_rot[yob] = vyb
            v_g1_dv[gb] = kb.val("sDV")
            v_ltused = kb.val("sDV")
        v_yz = kb.val("sDV")
        if DEV_DEBUG_OUT and z == 1:
            kb.wait("gpsimd", "sDV", v_yz)
            vdbg = kb.dma("gpsimd", t["dbg_yz"][:], t["y_z"][:, :], "sYS1")
            kb.wait("gpsimd", "sYS1", vdbg)
            kb.wait("vector", "sYS1", vdbg)
        aend = t["acfm"][:, 128 * z + 127:128 * z + 128]
        vexz = _sp_chain_act(kb, t["ecol"][:, :], aend, AF.Exp)
        kb.wait("gpsimd", "sSC", vexz)
        vesb = _ebcast(kb, t["ecol"][:, :], kb.state_comb_eb)
        tr_ev[6] = max(tr_ev[6], vesb)
        kb.wait("vector", "sSC", vesb)
        kb.wait("vector", "sSZ1", v_sld)
        for h in range(H):
            v_prevu = kb.dve(lambda e, h=h: e.scalar_tensor_tensor(
                out=t["prev"][:, 64 * h:64 * (h + 1)], in0=t["prev"][:, 64 * h:64 * (h + 1)],
                scalar=t["esb"][:, h:h + 1], in1=t["stzld"][:, 64 * h:64 * (h + 1)],
                op0=ALU.mult, op1=ALU.add), sem="sDV")
        kb.wait("vector", "sDV", v_prevu)
        for g in range(G):
            sl = slice(1024 * g, 1024 * (g + 1))
            vgw = kb.dve(lambda e, sl=sl: e.tensor_mul(out=t["gwork"][:, :], in0=t["y_z"][:, sl],
                                                       in1=t["gtch1"][:, sl]), sem="sDV")
            kb.wait("vector", "sDV", vgw)
            vsq = kb.dve(lambda e: e.tensor_mul(out=t["sqw"][:, :], in0=t["gwork"][:, :],
                                                in1=t["gwork"][:, :]), sem="sDV")
            kb.wait("vector", "sDV", vsq)
            vrs = kb.dve(lambda e, g=g: e.reduce_sum(out=t["grs"][:, g:g + 1], in_=t["sqw"][:, :],
                                                     axis=mybir.AxisListType.X), sem="sDV")
            kb.wait("vector", "sDV", vrs)
            vsc2 = kb.dve(lambda e, g=g: e.tensor_scalar(out=t["grs"][:, 8 + g:9 + g], in0=t["grs"][:, g:g + 1],
                                                         scalar1=1.0 / 1024.0, scalar2=GN_EPS,
                                                         op0=ALU.mult, op1=ALU.add), sem="sDV")
            kb.wait("scalar", "sDV", vsc2)
            vsq2 = kb.act(t["grs"][:, g:g + 1], t["grs"][:, 8 + g:9 + g], AF.Sqrt, sem="sSC")
            kb.wait("vector", "sSC", vsq2)
            vrc = kb.dve(lambda e, g=g: e.reciprocal(out=t["grs"][:, 8 + g:9 + g], in_=t["grs"][:, g:g + 1]),
                         sem="sDV")
            kb.wait("vector", "sDV", vrc)
            vyn = kb.dve(lambda e, g=g, sl=sl: e.tensor_scalar_mul(
                out=t["y_z"][:, sl], in0=t["gwork"][:, :], scalar1=t["grs"][:, 8 + g:9 + g]), sem="sDV")
            kb.wait("vector", "sDV", vyn)
        v_yn = kb.val("sDV")
        v_yn_prev = v_yn
        kb.wait("tensor", "sDV", v_yn)
        vlast = 0
        for tt in range(ET_OUT):
            tb = 6 + (tt % 2)
            kb.wait("tensor", "sSC", tr_ev[tb])
            vt = kb.trans(t["ps%d" % tb][:, 0:256].bitcast(BF16)[:, 0:128],
                          t["y_z"][:, 128 * tt:128 * (tt + 1)], t["ident_bf"][:], sem="sPE")
            kb.wait("scalar", "sPE", vt)
            tr_ev[tb] = vlast = kb.act(t["ystg"][:, 128 * tt:128 * (tt + 1)],
                                       t["ps%d" % tb][:, 0:256].bitcast(BF16)[:, 0:128],
                                       AF.Copy, scale=t["consts"][:, CO_NORMW + tt:CO_NORMW + tt + 1], sem="sSC")
        kb.state_scan_pe[0] = vt
        kb.wait("gpsimd", "sSC", vlast)
        zp, zz = z // 2, z % 2
        dstv = t["scr_yn"][:, zp, :, 128 * zz:128 * (zz + 1)].transpose([1, 0, 2])
        v_ystg_sp = kb.dma("gpsimd", dstv,
                           t["ystg"][:].rearrange("p (tt l) -> p tt l", l=128), "sYS0")
    kb.state_scan_sp = kb.val("sYS0")
    kb.state_scan_dv = kb.val("sDV")


def emit_outproj(kb: KB):
    t = kb.t
    kb.wait("gpsimd", "sYS0", kb.state_scan_sp)
    v_yl = 0
    for tt in range(ET_OUT):
        v_yl = kb.dma("gpsimd", t["yres"][:, 1024 * tt:1024 * (tt + 1)],
                      t["scr_yn"][tt].rearrange("zp p l -> p zp l"), "sYS1")
    rot_ev = {0: 0, 1: 0, 2: 0, 3: 0}
    ri = [0]
    v_wld = [0, 0]
    v_xld = [0] * ET_FFN
    v_stop = [0] * ET_FFN
    v_comb = [0] * ET_FFN
    v_out = [0] * ET_FFN
    kb.wait("tensor", "sYS1", v_yl)
    for ot in range(ET_FFN):
        par = ot % 2
        for half in range(2):
            if ot >= 1:
                kb.wait("gpsimd", "sPE", v_stop[ot - 1])
            v_wld[half] = kb.dma("gpsimd", t["wbuf"][:, half],
                                 t["outproj_wT"][4096 * half:4096 * (half + 1),
                                                 128 * ot:128 * (ot + 1)].rearrange("(t p) e -> p t e", p=128),
                                 "sWB%d" % half)
        if ot >= 2:
            kb.wait("scalar", "sDV", v_comb[ot - 2])
        v_xld[ot] = kb.dma("scalar", t["oxres"][:, par, :], t["scr_x"][128 * ot:128 * (ot + 1), :],
                           "sXR%d" % par)
        banks = []
        v = 0
        for ci, off in enumerate((0, 512)):
            rb = ri[0] % 4
            ri[0] += 1
            banks.append(rb)
            kb.wait("tensor", "sSC", rot_ev[rb])
            for dd in range(64):
                half = dd // 32
                if dd == 0:
                    kb.wait("tensor", "sWB0", v_wld[0])
                if dd == 32:
                    kb.wait("tensor", "sWB1", v_wld[1])
                lhs = t["wbuf"][:, half, dd % 32, :]
                v = kb.mm(t["ps%d" % rb][:, 0:512], lhs,
                          t["yres"][:, 1024 * dd + off:1024 * dd + off + 512],
                          dd == 0, dd == 63, sem="sPE" if (ci == 1 and dd == 63) else None)
        v_stop[ot] = v
        kb.wait("scalar", "sPE", v_stop[ot])
        kb.wait("scalar", "sXR%d" % par, v_xld[ot])
        b0, b1 = banks
        rot_ev[b0] = kb.act(t["oxp"][:, par, 3:515], t["ps%d" % b0][:, 0:512], AF.Copy, sem="sSC")
        rot_ev[b1] = vsc = kb.act(t["oxp"][:, par, 515:1027], t["ps%d" % b1][:, 0:512], AF.Copy, sem="sSC")
        kb.wait("vector", "sSC", vsc)
        kb.dve(lambda e, par=par: e.tensor_copy(out=t["oxp"][:, par, 0:3], in_=t["oxres"][:, par, 0:3]),
               sem="sDV")
        v_comb[ot] = kb.dve(lambda e, par=par: e.tensor_add(
            out=t["oxp"][:, par, 3:1027], in0=t["oxp"][:, par, 3:1027], in1=t["oxres"][:, par, 3:1027]),
            sem="sDV")
        kb.wait("gpsimd", "sDV", v_comb[ot])
        if ot >= 2:
            kb.wait("gpsimd", "sXP%d" % par, v_out[ot - 2])
        v_out[ot] = kb.dma("gpsimd", t["scr_x"][128 * ot:128 * (ot + 1), :], t["oxp"][:, par, :],
                           "sXP%d" % par)
    kb.state_ffn_xp0 = kb.val("sXP0")
    kb.state_ffn_xp1 = kb.val("sXP1")
    kb.state_ffn_pe = kb.val("sPE")


def emit_dbg_copy(kb: KB, src, dst):
    t = kb.t
    kb.wait("sync", "sGP", kb.state_ffn_gp)
    kb.state_dbg = kb.dma("sync", t[dst][:], t[src][:], "sWL")


def emit_final(kb: KB, mode):
    t = kb.t
    kb.wait("sync", "sXP0", kb.val("sXP0"))
    kb.wait("sync", "sXP1", kb.val("sXP1"))
    for s in ("sSP0", "sSP1", "sSZ0", "sSZ1", "sWC", "sSO", "sYS0", "sYS1"):
        kb.wait("sync", s, kb.val(s))
    if mode == "L2":
        v = kb.dma("sync", t["scr_row"][3:4, 0:8], t["consts"][0:1, 0:8], "sST")
    else:
        v = kb.dma("sync", t["out_fm"][:], t["scr_x"][:, HALO:T], "sST")
    if DEV_DEBUG_OUT:
        kb.dma("sync", t["dbg_hs"][:], t["halfscale"][:], "sST")
        kb.dma("sync", t["dbg_rows"][0:1, :], t["mrow"][0:1, :], "sST")
        kb.dma("sync", t["dbg_rows"][1:2, :], t["rsrow"][0:1, :], "sST")
        kb.dma("gpsimd", t["dbg_u"][:], t["u_bf"][:], "sST")
        kb.dma("sync", t["dbg_mb"][:], t["mb"][:], "sST")
        kb.dma("sync", t["dbg_rb"][:], t["rb"][:], "sST")
        kb.dma("sync", t["dbg_t1"][:], t["t1col_sb"][:], "sST")
        kb.dma("sync", t["dbg_wat"][:], t["watcol_sb"][:], "sST")
        v = kb.dma("gpsimd", t["dbg_xn"][:], t["xn"][:], "sST")
        kb.wait("gpsimd", "sST", v)
    kb.wait("sync", "sST", v)


def kernel(**inputs):
    per_core = host_prep(inputs)
    nc1, _ = build_program("L1")
    res1 = run_bass_kernel_spmd(nc1, per_core, core_ids=list(range(NCORES)))
    states = np.stack([np.asarray(res1.results[c]["out_state"]) for c in range(NCORES)])
    dsegs = np.concatenate([np.asarray(res1.results[c]["out_dseg"]) for c in range(NCORES)], axis=1)
    nc2, _ = build_program("L2")
    maps2 = []
    RELAY = ("scr_x", "scr_hs_tm", "scr_gate_tm", "scr_B_fm", "scr_C_fm",
             "scr_acum", "scr_actm", "scr_states", "scr_dt", "scr_wb", "scr_hscale")
    for c in range(NCORES):
        m = dict(per_core[c])
        for k in RELAY:
            m[k] = np.asarray(res1.results[c][k])
        m["state_all"] = states
        m["dseg_all"] = dsegs
        sm = np.zeros((1, NCORES), dtype=np.float32)
        sm[0, :c] = 1.0
        m["segmask"] = sm
        maps2.append(m)
    res2 = run_bass_kernel_spmd(nc2, maps2, core_ids=list(range(NCORES)))
    out = np.zeros((D, S), dtype=np.float32)
    for c in range(NCORES):
        out[:, TOK * c:TOK * (c + 1)] = res2.results[c]["out_fm"]
    return np.ascontiguousarray(out.T).reshape(1, S, D)
